# version 02
# baseline (speedup 1.0000x reference)
import numpy as np
import concourse.bass as bass
import concourse.mybir as mybir
from concourse.bass_utils import run_bass_kernel_spmd

F32 = mybir.dt.float32
AF = mybir.ActivationFunctionType
ALU = mybir.AluOpType
AX = mybir.AxisListType

NEG_BIG = -1.0e30


class Cfg:
    def __init__(self, T=4096, DFF=4096):
        self.DM = 1024
        self.T = T
        self.DFF = DFF
        self.RANK = 256
        self.NPAIR = 8
        self.OWN = T // 2
        self.NT = T // 128
        self.SLOTS = self.OWN // 128
        self.G = self.SLOTS // 4
        self.CCH = self.DM // 128
        self.FC = DFF // 128
        self.NT_OWN = self.OWN // 128
        assert self.SLOTS % 4 == 0 and self.OWN % 1024 == 0


class B:

    def __init__(self, cfg):
        self.cfg = cfg
        self.nc = bass.Bass("TRN2", target_bir_lowering=False)
        c = cfg
        dt = self.nc.dram_tensor
        self.d_xf = dt("x_full", [c.T, c.DM], F32, kind="ExternalInput").ap()
        self.d_xo = dt("x_own", [c.OWN, c.DM], F32, kind="ExternalInput").ap()
        self.d_qkvw = dt("qkv_w", [3 * c.DM, c.DM], F32, kind="ExternalInput").ap()
        self.d_qkvb = dt("qkv_b2d", [128, 3 * c.DM // 128], F32, kind="ExternalInput").ap()
        self.d_ow = dt("out_w", [c.DM, c.DM], F32, kind="ExternalInput").ap()
        self.d_ob = dt("out_b2d", [1, c.DM], F32, kind="ExternalInput").ap()
        self.d_u1 = dt("U1", [c.DM, c.RANK], F32, kind="ExternalInput").ap()
        self.d_v1 = dt("V1", [c.RANK, c.DFF], F32, kind="ExternalInput").ap()
        self.d_b1 = dt("b1_2d", [128, c.FC], F32, kind="ExternalInput").ap()
        self.d_u2 = dt("U2", [c.DFF, c.RANK], F32, kind="ExternalInput").ap()
        self.d_v2 = dt("V2", [c.RANK, c.DM], F32, kind="ExternalInput").ap()
        self.d_b2 = dt("b2_2d", [1, c.DM], F32, kind="ExternalInput").ap()
        self.d_ident = dt("c_ident", [128, 128], F32, kind="ExternalInput").ap()
        self.d_ones = dt("c_ones", [128, 128], F32, kind="ExternalInput").ap()
        self.d_masks = dt("c_masks", [128, 256], F32, kind="ExternalInput").ap()
        self.d_y = dt("y_own", [c.OWN, c.DM], F32, kind="ExternalOutput").ap()
        self.d_attT = dt("attT_scr", [c.G, c.DM, 512], F32, kind="Internal").ap()

        al = self.nc.alloc_sbuf_tensor
        self.ident = al("ident", [128, 128], F32).ap()
        self.ones = al("ones", [128, 128], F32).ap()
        self.masks = al("masks", [128, 256], F32).ap()
        self.qkvb = al("qkvb", [128, 3 * c.DM // 128], F32).ap()
        self.mneg = al("mneg", [128, c.NPAIR], F32).ap()
        self.stat = al("stat", [128, 32], F32).ap()
        self.rs = al("rs", [1, 1024], F32).ap()
        self.AR = 50600
        self.arena = al("arena", [128, self.AR], F32).ap()
        self.off = 0

        self.pb = [self.nc.alloc_psum_tensor(f"pb{i}", [128, 512], F32).ap()
                   for i in range(8)]
        self.bguard = {i: None for i in range(8)}

        self.sems = {}
        self.cnt = {}
        self.ops = {e: [] for e in ("sync", "tensor", "vector", "scalar")}
        self._pend = {}
        self._semctx = []
        self.dpool = {"in": [f"s_i{j}" for j in range(4)],
                      "sp": [f"s_p{j}" for j in range(2)],
                      "y": [f"s_y{j}" for j in range(2)]}
        self.drot = {k: 0 for k in self.dpool}
        semnames = ["s_t", "s_v", "s_a"]
        for pool in self.dpool.values():
            semnames += pool
        for s in semnames:
            ctx = self.nc.semaphore(s)
            self.sems[s] = ctx.__enter__()
            self._semctx.append(ctx)
            self.cnt[s] = 0

    def sub(self, n, inner=None):
        a = self.arena[:, self.off:self.off + n]
        self.off += n
        assert self.off <= self.AR, f"arena overflow: {self.off} > {self.AR}"
        if inner is not None:
            a = a.rearrange("p (a b) -> p a b", b=inner)
        return a

    def _waitraw(self, eng, name, val):
        if val <= 0:
            return
        sem = self.sems[name]
        self.ops[eng].append(lambda e, sem=sem, val=val: e.wait_ge(sem, val))

    def emit(self, eng, fn):
        if eng == "vector":
            assert not self._pend.get(eng), "vector op emitted without inc_tag"
            self._waitraw(eng, "s_v", self.cnt["s_v"])
            self._pend[eng] = True
        elif eng == "scalar":
            assert not self._pend.get(eng), "scalar op emitted without inc_tag"
            self._waitraw(eng, "s_a", self.cnt["s_a"])
            self._pend[eng] = True
        self.ops[eng].append(fn)

    def wait(self, eng, name, val):
        assert val <= self.cnt[name], f"wait-on-future {name} {val}>{self.cnt[name]}"
        self._waitraw(eng, name, val)

    def wait_g(self, eng, guard):
        if guard is not None:
            self.wait(eng, guard[0], guard[1])

    def inc_tag(self, eng, name, n=1):
        self.cnt[name] += n
        sem, val = self.sems[name], self.cnt[name]
        fn = self.ops[eng].pop()
        self.ops[eng].append(lambda e, fn=fn, sem=sem, n=n: fn(e).then_inc(sem, n))
        self._pend[eng] = False
        return val

    def dma(self, stream, fn, eng="sync"):
        pool = self.dpool[stream]
        j = self.drot[stream]
        self.drot[stream] = (j + 1) % len(pool)
        s = pool[j]
        self.wait(eng, s, self.cnt[s])
        self.emit(eng, fn)
        v = self.inc_tag(eng, s, 16)
        return (s, v)

    def wait_stream(self, eng, stream):
        for s in self.dpool[stream]:
            self.wait(eng, s, self.cnt[s])

    def build(self):
        c = self.cfg
        self.xT = self.sub(c.CCH * c.T, c.T)
        self.wt = self.sub(3 * c.DM, 128)
        self.qt = self.sub(c.OWN)
        self.kt = self.sub(c.T)
        self.vv = self.sub(c.NT * 130, 130)
        self.at = self.sub(2048, 512)
        self.wbuf = self.sub(2048)
        att_end = self.off

        self.prep()
        for p in range(c.NPAIR):
            self.pair_proj(p)
            self.pair_m(p)
            self.pair_attn(p)

        self.off = 0
        self.x1tm = self.sub(c.NT_OWN * c.DM, c.DM)
        self.x1T = self.sub(c.CCH * c.OWN, c.OWN)
        ph34 = self.off
        self.woT = self.sub(c.CCH * c.DM, c.DM)
        self.attsb = self.sub(2 * c.CCH * 128, 128)
        self.xo = self.sub(2 * c.DM, c.DM)
        self.lnx = self.sub(c.DM)
        self.lnsq = self.sub(c.DM)
        self.wstage = self.sub(c.DM)
        self.obs = self.sub(c.DM)
        p3_end = self.off
        self.phase3()

        self.off = ph34
        self.lnx4 = self.sub(c.DM)
        self.lnsq4 = self.sub(c.DM)
        self.u1s = self.sub(c.CCH * c.RANK, c.RANK)
        self.v2s = self.sub(2 * c.DM, c.DM)
        self.v1st = self.sub(2 * 256, 256)
        self.u2st = self.sub(2 * 256, 256)
        self.ptall = self.sub(2 * c.OWN, c.OWN)
        self.hts = self.sub(2 * 512, 512)
        self.q2t = self.sub(2 * 512, 512)
        self.ytile = self.sub(2 * c.DM, c.DM)
        self.b1s = self.sub(c.FC)
        self.b2s = self.sub(c.DM)
        p4_end = self.off
        self.phase4()
        self.sbuf_free = self.AR - max(att_end, p3_end, p4_end)

        with self.nc.Block() as block:
            @block.sync
            def _(e):
                for fn in self.ops["sync"]:
                    fn(e)

            @block.tensor
            def _(e):
                for fn in self.ops["tensor"]:
                    fn(e)

            @block.vector
            def _(e):
                for fn in self.ops["vector"]:
                    fn(e)

            @block.scalar
            def _(e):
                for fn in self.ops["scalar"]:
                    fn(e)
        for ctx in self._semctx:
            ctx.__exit__(None, None, None)
        return self.nc

    def transpose_block(self, src, dst_fn, nchunks=8):
        tlast = 0
        for ch in range(nchunks):
            bi = 6 + (ch % 2)
            bank = self.pb[bi]
            self.wait_g("tensor", self.bguard[bi])
            self.emit("tensor",
                      lambda e, src=src, ch=ch, bank=bank: e.transpose(
                          bank[:, 0:128], src[:, ch * 128:(ch + 1) * 128], self.ident))
            tlast = self.inc_tag("tensor", "s_t")
            self.wait("vector", "s_t", tlast)
            self.emit("vector",
                      lambda e, bank=bank, dst=dst_fn(ch): e.tensor_copy(dst, bank[:, 0:128]))
            v = self.inc_tag("vector", "s_v")
            self.bguard[bi] = ("s_v", v)
        return tlast

    def prep(self):
        c = self.cfg
        g_ident = self.dma("in", lambda e: e.dma_start(out=self.ident, in_=self.d_ident))
        g_ones = self.dma("in", lambda e: e.dma_start(out=self.ones, in_=self.d_ones))
        self.g_masks = self.dma("in", lambda e: e.dma_start(out=self.masks, in_=self.d_masks))
        self.g_qkvb = self.dma("in", lambda e: e.dma_start(out=self.qkvb, in_=self.d_qkvb))
        self.emit("vector", lambda e: e.memset(self.stat[:, 16:17], 1.0e-5))
        self.inc_tag("vector", "s_v")
        self.wait_g("tensor", g_ident)
        self.wait_g("tensor", g_ones)
        xin = (self.wbuf[:, 0:1024], self.wbuf[:, 1024:2048])
        xin_guard = [None, None]
        for ti in range(c.NT):
            dbi = ti % 2
            self.wait_g("sync", xin_guard[dbi])
            gx = self.dma("in", lambda e, ti=ti, x=xin[dbi]: e.dma_start(
                out=x, in_=self.d_xf[ti * 128:(ti + 1) * 128, :]))
            self.wait_g("tensor", gx)
            tlast = self.transpose_block(
                xin[dbi], lambda ch, ti=ti: self.xT[:, ch, ti * 128:(ti + 1) * 128])
            xin_guard[dbi] = ("s_t", tlast)

    def pair_proj(self, p):
        c = self.cfg
        self.wait_g("sync", getattr(self, "last_sp", None))
        self.wait("sync", "s_t", self.cnt["s_t"])

        def load_tr(row0, wtbase, soff):
            st = self.wbuf[:, soff:soff + 1024]
            gw = self.dma("in", lambda e, st=st, row0=row0: e.dma_start(
                out=st, in_=self.d_qkvw[row0:row0 + 128, :]))
            self.wait_g("tensor", gw)
            return self.transpose_block(
                st, lambda ch, wtbase=wtbase: self.wt[:, wtbase + ch, :])

        if p == 0:
            self.wait_g("scalar", self.g_qkvb)
        t_q = load_tr(128 * p, 0, 0)
        t_k = load_tr(c.DM + 128 * p, c.CCH, 1024)
        self.wait("sync", "s_t", t_q)
        load_tr(2 * c.DM + 128 * p, 2 * c.CCH, 0)

        self.wait("vector", "s_t", self.cnt["s_t"])
        self.emit("vector", lambda e: e.memset(self.vv[:, :, 64:65], 1.0))
        self.inc_tag("vector", "s_v")
        self.emit("vector", lambda e: e.memset(self.vv[:, :, 129:130], 1.0))
        self.inc_tag("vector", "s_v")

        self.wait("tensor", "s_v", self.cnt["s_v"])
        for tc in range(c.OWN // 512):
            bi = tc % 2
            bank = self.pb[bi]
            self.wait_g("tensor", self.bguard[bi])
            for ch in range(c.CCH):
                for s in range(4):
                    col0 = 2 * (4 * tc + s) * 128
                    self.emit("tensor",
                              lambda e, bank=bank, ch=ch, s=s, col0=col0,
                              st=(ch == 0 and s == 0), sp=(ch == c.CCH - 1 and s == 3):
                              e.matmul(bank[:, s * 128:(s + 1) * 128],
                                       self.wt[:, ch, :],
                                       self.xT[:, ch, col0:col0 + 128],
                                       start=st, stop=sp))
            tmm = self.inc_tag("tensor", "s_t")
            self.wait("scalar", "s_t", tmm)
            self.emit("scalar", lambda e, bank=bank, tc=tc, p=p: e.activation(
                out=self.qt[:, tc * 512:(tc + 1) * 512], in_=bank[:, :],
                func=AF.Identity, bias=self.qkvb[:, p:p + 1], scale=1.0))
            av = self.inc_tag("scalar", "s_a")
            self.bguard[bi] = ("s_a", av)

        for tc in range(c.T // 512):
            bi = 2 + (tc % 2)
            bank = self.pb[bi]
            self.wait_g("tensor", self.bguard[bi])
            for ch in range(c.CCH):
                self.emit("tensor", lambda e, bank=bank, ch=ch, tc=tc:
                          e.matmul(bank[:, :], self.wt[:, c.CCH + ch, :],
                                   self.xT[:, ch, tc * 512:(tc + 1) * 512],
                                   start=(ch == 0), stop=(ch == c.CCH - 1)))
            tmm = self.inc_tag("tensor", "s_t")
            self.wait("scalar", "s_t", tmm)
            self.emit("scalar", lambda e, bank=bank, tc=tc, p=p: e.activation(
                out=self.kt[:, tc * 512:(tc + 1) * 512], in_=bank[:, :],
                func=AF.Identity, bias=self.qkvb[:, c.CCH + p:c.CCH + p + 1], scale=1.0))
            av = self.inc_tag("scalar", "s_a")
            self.bguard[bi] = ("s_a", av)

        vt_stage = self.at[:, 0, :]
        for tc in range(c.T // 512):
            bi = 4 + (tc % 2)
            bank = self.pb[bi]
            self.wait_g("tensor", self.bguard[bi])
            for ch in range(c.CCH):
                self.emit("tensor", lambda e, bank=bank, ch=ch, tc=tc:
                          e.matmul(bank[:, :], self.wt[:, 2 * c.CCH + ch, :],
                                   self.xT[:, ch, tc * 512:(tc + 1) * 512],
                                   start=(ch == 0), stop=(ch == c.CCH - 1)))
            tmm = self.inc_tag("tensor", "s_t")
            self.wait("scalar", "s_t", tmm)
            self.emit("scalar", lambda e, bank=bank, p=p: e.activation(
                out=vt_stage, in_=bank[:, :],
                func=AF.Identity, bias=self.qkvb[:, 2 * c.CCH + p:2 * c.CCH + p + 1], scale=1.0))
            a_ev = self.inc_tag("scalar", "s_a")
            self.bguard[bi] = ("s_a", a_ev)
            for j in range(4):
                tbi = 6 + (j % 2)
                tb = self.pb[tbi]
                self.wait("tensor", "s_a", a_ev)
                self.wait_g("tensor", self.bguard[tbi])
                self.emit("tensor", lambda e, j=j, tb=tb, vt=vt_stage: e.transpose(
                    tb[:, 0:128], vt[:, j * 128:(j + 1) * 128], self.ident))
                tlast = self.inc_tag("tensor", "s_t")
                self.wait("vector", "s_t", tlast)
                self.emit("vector", lambda e, tb=tb, tc=tc, j=j: e.tensor_copy(
                    self.vv[:, 4 * tc + j, :].rearrange("p (h q) -> p h q", h=2)[:, :, 0:64],
                    tb[:, 0:128].rearrange("p (h d) -> p h d", h=2)))
                v = self.inc_tag("vector", "s_v")
                self.bguard[tbi] = ("s_v", v)
            self.wait("scalar", "s_t", self.cnt["s_t"])
        self._proj_a = self.cnt["s_a"]
        self._proj_v = self.cnt["s_v"]

    def pair_m(self, p):
        c = self.cfg
        self.wait("vector", "s_t", self.cnt["s_t"])
        self.wait("vector", "s_a", self.cnt["s_a"])
        col = 0
        for (src, nch) in ((self.qt, c.OWN // 512), (self.kt, c.T // 512)):
            for tc in range(nch):
                sq = self.at[:, tc % 4, :]
                if col >= 4:
                    self.wait("vector", "s_t", self.cnt["s_t"])
                self.emit("vector", lambda e, src=src, tc=tc, sq=sq: e.tensor_mul(
                    sq, src[:, tc * 512:(tc + 1) * 512], src[:, tc * 512:(tc + 1) * 512]))
                sqv = self.inc_tag("vector", "s_v")
                bi = 6 + (col % 2)
                bank = self.pb[bi]
                self.wait("tensor", "s_v", sqv)
                self.wait_g("tensor", self.bguard[bi])
                self.emit("tensor", lambda e, bank=bank, sq=sq: e.matmul(
                    bank[:, :], self.ones[:, 0:128], sq, start=True, stop=True))
                tv = self.inc_tag("tensor", "s_t")
                self.wait("vector", "s_t", tv)
                self.emit("vector", lambda e, bank=bank, col=col: e.reduce_max(
                    self.stat[0:1, col:col + 1], bank[0:1, :], axis=AX.X))
                v = self.inc_tag("vector", "s_v")
                self.bguard[bi] = ("s_v", v)
                col += 1
        nq = c.OWN // 512
        nk = c.T // 512
        self.emit("vector", lambda e, nq=nq: e.reduce_max(
            self.stat[0:1, 28:29], self.stat[0:1, 0:nq], axis=AX.X))
        self.inc_tag("vector", "s_v")
        self.emit("vector", lambda e, nq=nq, nk=nk: e.reduce_max(
            self.stat[0:1, 29:30], self.stat[0:1, nq:nq + nk], axis=AX.X))
        vmax = self.inc_tag("vector", "s_v")
        self.wait("scalar", "s_v", vmax)
        self.emit("scalar", lambda e: e.activation(
            out=self.stat[0:1, 30:31], in_=self.stat[0:1, 28:29], func=AF.Sqrt,
            bias=0.0, scale=1.0))
        self.inc_tag("scalar", "s_a")
        self.emit("scalar", lambda e: e.activation(
            out=self.stat[0:1, 31:32], in_=self.stat[0:1, 29:30], func=AF.Sqrt,
            bias=0.0, scale=1.0))
        sa = self.inc_tag("scalar", "s_a")
        self.wait("vector", "s_a", sa)
        self.emit("vector", lambda e: e.tensor_mul(
            self.stat[0:1, 28:29], self.stat[0:1, 30:31], self.stat[0:1, 31:32]))
        vm = self.inc_tag("vector", "s_v")
        self.wait("scalar", "s_v", vm)
        self.emit("scalar", lambda e: e.activation(
            out=self.stat[0:1, 29:30], in_=self.stat[0:1, 28:29], func=AF.Copy,
            bias=0.0, scale=-0.125))
        sa = self.inc_tag("scalar", "s_a")
        bi = 6
        bank = self.pb[bi]
        self.wait("tensor", "s_a", sa)
        self.wait_g("tensor", self.bguard[bi])
        self.emit("tensor", lambda e, bank=bank: e.matmul(
            bank[:, 0:1], self.ones[0:1, :], self.stat[0:1, 29:30],
            start=True, stop=True))
        tv = self.inc_tag("tensor", "s_t")
        self.wait("vector", "s_t", tv)
        self.emit("vector", lambda e, p=p, bank=bank: e.tensor_copy(
            self.mneg[:, p:p + 1], bank[:, 0:1]))
        v = self.inc_tag("vector", "s_v")
        self.bguard[bi] = ("s_v", v)
        self._mneg_v = v

    def pair_attn(self, p):
        c = self.cfg
        attT_tile = self.wbuf[:, 0:512]
        rbc = self.wbuf[:, 512:1024]
        at_guard = {}
        self.wait("tensor", "s_a", self._proj_a)
        self.wait("tensor", "s_v", self._proj_v)
        if p == 0:
            self.wait_g("vector", self.g_masks)
        mneg_waited = False

        def emit_qk_exp(g, cc):
            nonlocal mneg_waited
            s0 = max(4 * g, cc // 2)
            qoff = 128 * (s0 - 4 * g)
            par = cc % 2
            for h in range(2):
                sbi = 2 * h + par
                sb = self.pb[sbi]
                self.wait_g("tensor", self.bguard[sbi])
                self.emit("tensor", lambda e, sb=sb, h=h, cc=cc, g=g, qoff=qoff:
                          e.matmul(sb[:, qoff:512],
                                   self.kt[64 * h:64 * h + 64, cc * 128:(cc + 1) * 128],
                                   self.qt[64 * h:64 * h + 64, 512 * g + qoff:512 * (g + 1)],
                                   start=True, stop=True))
                tqk = self.inc_tag("tensor", "s_t")
                masked = (cc >= 8 * g)
                if masked:
                    wcol = 128 * (cc // 2 - 4 * g)
                    self.wait("vector", "s_t", tqk)
                    self.emit("vector", lambda e, sb=sb, wcol=wcol, par=par:
                              e.tensor_tensor(sb[:, wcol:wcol + 128],
                                              sb[:, wcol:wcol + 128],
                                              self.masks[:, par * 128:(par + 1) * 128],
                                              ALU.add))
                    vmask = self.inc_tag("vector", "s_v")
                    self.wait("scalar", "s_v", vmask)
                else:
                    self.wait("scalar", "s_t", tqk)
                if not mneg_waited:
                    self.wait("scalar", "s_v", self._mneg_v)
                    mneg_waited = True
                self.wait_g("scalar", at_guard.get((h, par)))
                self.emit("scalar", lambda e, sb=sb, h=h, par=par, qoff=qoff, p=p:
                          e.activation(out=self.at[:, 2 * h + par, qoff:512],
                                       in_=sb[:, qoff:512], func=AF.Exp,
                                       bias=self.mneg[:, p:p + 1], scale=0.125))
                aexp = self.inc_tag("scalar", "s_a")
                self.bguard[sbi] = ("s_a", aexp)

        def emit_av(g, cc, o_banks):
            s0 = max(4 * g, cc // 2)
            qoff = 128 * (s0 - 4 * g)
            par = cc % 2
            for h in range(2):
                sbi = 2 * h + par
                self.wait("tensor", "s_a", self.bguard[sbi][1])
                if cc == 0:
                    self.wait_g("tensor", self.bguard[4 + h])
                self.emit("tensor", lambda e, h=h, par=par, cc=cc, qoff=qoff,
                          ob=o_banks[h], g=g:
                          e.matmul(ob[0:65, qoff:512],
                                   self.vv[:, cc, 65 * h:65 * h + 65],
                                   self.at[:, 2 * h + par, qoff:512],
                                   start=(cc == 0), stop=(cc == 8 * g + 7)))
                at_guard[(h, par)] = ("s_t", self.inc_tag("tensor", "s_t"))

        def emit_evac(g, o_banks, t_flush):
            self.wait("vector", "s_t", t_flush)
            self.wait_g("vector", getattr(self, "last_sp", None))
            vr = 0
            for h in range(2):
                self.emit("vector", lambda e, h=h, ob=o_banks[h]: e.reciprocal(
                    self.rs[0:1, 512 * h:512 * h + 512], ob[64:65, 0:512]))
                vr = self.inc_tag("vector", "s_v")
            for h in range(2):
                rbi = 6 + h
                rb = self.pb[rbi]
                self.wait("tensor", "s_v", vr)
                self.wait_g("tensor", self.bguard[rbi])
                self.emit("tensor", lambda e, rb=rb, h=h: e.matmul(
                    rb[:, 0:512], self.ones[0:1, :],
                    self.rs[0:1, 512 * h:512 * h + 512], start=True, stop=True))
                trb = self.inc_tag("tensor", "s_t")
                self.wait("vector", "s_t", trb)
                self.emit("vector", lambda e, rb=rb: e.tensor_copy(
                    rbc[0:64, :], rb[0:64, 0:512]))
                vv_ = self.inc_tag("vector", "s_v")
                self.bguard[rbi] = ("s_v", vv_)
                self.emit("vector", lambda e, h=h, ob=o_banks[h], at_t=attT_tile:
                          e.tensor_tensor(at_t[64 * h:64 * h + 64, :],
                                          ob[0:64, 0:512], rbc[0:64, :], ALU.mult))
                v = self.inc_tag("vector", "s_v")
                self.bguard[4 + h] = ("s_v", v)
            self.wait("sync", "s_v", self.cnt["s_v"])
            self.last_sp = self.dma("sp", lambda e, p=p, g=g, at_t=attT_tile: e.dma_start(
                out=self.d_attT[g, 128 * p:128 * (p + 1), :], in_=at_t))

        pending = None
        o_banks = (self.pb[4], self.pb[5])
        for g in range(c.G):
            cmax = 8 * g + 8
            for cc in range(cmax):
                emit_qk_exp(g, cc)
                if cc == 1 and pending is not None:
                    emit_evac(*pending)
                    pending = None
                if cc >= 1:
                    emit_av(g, cc - 1, o_banks)
            emit_av(g, cmax - 1, o_banks)
            pending = (g, o_banks, self.cnt["s_t"])
        emit_evac(*pending)

    def phase3(self):
        c = self.cfg
        self.wait("sync", "s_t", self.cnt["s_t"])
        self.wait("sync", "s_a", self.cnt["s_a"])
        self.wait("sync", "s_v", self.cnt["s_v"])
        self.wait_stream("sync", "sp")
        for wc in range(c.CCH):
            gw = self.dma("in", lambda e, wc=wc: e.dma_start(
                out=self.wstage, in_=self.d_ow[128 * wc:128 * (wc + 1), :]))
            self.wait_g("tensor", gw)
            tlast = self.transpose_block(
                self.wstage, lambda ch, wc=wc: self.woT[:, ch, 128 * wc:128 * (wc + 1)])
            self.wait("sync", "s_t", tlast)
        g_obs = self.dma("in", lambda e: e.dma_start(out=self.obs[0:1, :], in_=self.d_ob))
        self.wait_g("tensor", g_obs)
        self.wait("tensor", "s_v", self.cnt["s_v"])

        asb_guard = [None, None]
        xo_guard = [None, None]
        for ti in range(c.NT_OWN):
            g, w, dbi = ti // 4, ti % 4, ti % 2
            asb = self.attsb[:, dbi * c.CCH:(dbi + 1) * c.CCH, :]
            self.wait_g("sync", asb_guard[dbi])
            ga = self.dma("in", lambda e, g=g, w=w, asb=asb: e.dma_start(
                out=asb,
                in_=self.d_attT[g, :, 128 * w:128 * (w + 1)].rearrange(
                    "(a p) b -> p a b", p=128)))
            self.wait_g("sync", xo_guard[dbi])
            gx = self.dma("in", lambda e, ti=ti, dbi=dbi: e.dma_start(
                out=self.xo[:, dbi, :], in_=self.d_xo[128 * ti:128 * (ti + 1), :]))
            self.wait_g("tensor", ga)
            self.wait_g("tensor", gx)
            tmm = 0
            for dk in range(2):
                bank = self.pb[dk]
                self.wait_g("tensor", self.bguard[dk])
                for ch in range(c.CCH):
                    self.emit("tensor", lambda e, bank=bank, ch=ch, dk=dk, asb=asb:
                              e.matmul(bank[:, :], asb[:, ch, :],
                                       self.woT[:, ch, 512 * dk:512 * (dk + 1)],
                                       start=(ch == 0), stop=False))
                self.emit("tensor", lambda e, bank=bank, dk=dk:
                          e.matmul(bank[:, :], self.ones[0:1, :],
                                   self.obs[0:1, 512 * dk:512 * (dk + 1)],
                                   start=False, stop=True))
                tmm = self.inc_tag("tensor", "s_t")
                self.wait("vector", "s_t", tmm)
                self.emit("vector", lambda e, bank=bank, dk=dk, dbi=dbi, lnx=self.lnx:
                          e.tensor_tensor(lnx[:, 512 * dk:512 * (dk + 1)],
                                          bank[:, :], self.xo[:, dbi, 512 * dk:512 * (dk + 1)],
                                          ALU.add))
                v = self.inc_tag("vector", "s_v")
                self.bguard[dk] = ("s_v", v)
            asb_guard[dbi] = ("s_t", tmm)
            xo_guard[dbi] = ("s_v", self.cnt["s_v"])
            self.layernorm(self.lnx, self.x1tm[:, ti, :], ti, self.x1T)

    def layernorm(self, xap, out_tm, ti, to_T, sq=None):
        c = self.cfg
        sq = sq if sq is not None else self.lnsq
        st = self.stat
        self.emit("vector", lambda e, xap=xap: e.reduce_sum(st[:, 0:1], xap, axis=AX.X))
        self.inc_tag("vector", "s_v")
        self.emit("vector", lambda e: e.tensor_scalar_mul(st[:, 1:2], st[:, 0:1], 1.0 / c.DM))
        self.inc_tag("vector", "s_v")
        self.emit("vector", lambda e, xap=xap: e.tensor_scalar(
            xap, xap, st[:, 1:2], None, ALU.subtract))
        v1 = self.inc_tag("vector", "s_v")
        self.wait("scalar", "s_v", v1)
        self.emit("scalar", lambda e, xap=xap, sq=sq: e.activation(
            out=sq, in_=xap, func=AF.Square, bias=0.0, scale=1.0))
        a1 = self.inc_tag("scalar", "s_a")
        self.wait("vector", "s_a", a1)
        self.emit("vector", lambda e, sq=sq: e.reduce_sum(st[:, 2:3], sq, axis=AX.X))
        v1 = self.inc_tag("vector", "s_v")
        self.wait("scalar", "s_v", v1)
        self.emit("scalar", lambda e: e.activation(
            out=st[:, 3:4], in_=st[:, 2:3], func=AF.Sqrt, bias=st[:, 16:17],
            scale=1.0 / c.DM))
        a1 = self.inc_tag("scalar", "s_a")
        self.wait("vector", "s_a", a1)
        self.emit("vector", lambda e: e.reciprocal(st[:, 4:5], st[:, 3:4]))
        self.inc_tag("vector", "s_v")
        self.emit("vector", lambda e, xap=xap, out_tm=out_tm: e.tensor_scalar(
            out_tm, xap, st[:, 4:5], None, ALU.mult))
        vnorm = self.inc_tag("vector", "s_v")
        if to_T is not None:
            self.wait("tensor", "s_v", vnorm)
            self.transpose_block(
                out_tm, lambda ch, ti=ti, to_T=to_T: to_T[:, ch, 128 * ti:128 * (ti + 1)])
        return vnorm

    def phase4(self):
        c = self.cfg
        self.wait("sync", "s_t", self.cnt["s_t"])
        self.wait("sync", "s_v", self.cnt["s_v"])
        for wc in range(c.CCH):
            self.dma("in", lambda e, wc=wc: e.dma_start(
                out=self.u1s[:, wc, :], in_=self.d_u1[128 * wc:128 * (wc + 1), :]))
        for rc in range(2):
            self.dma("in", lambda e, rc=rc: e.dma_start(
                out=self.v2s[:, rc, :], in_=self.d_v2[128 * rc:128 * (rc + 1), :]))
        self.dma("in", lambda e: e.dma_start(out=self.b1s, in_=self.d_b1))
        self.dma("in", lambda e: e.dma_start(out=self.b2s[0:1, :], in_=self.d_b2))
        self.wait_stream("tensor", "in")
        self.wait("tensor", "s_v", self.cnt["s_v"])

        for tc in range(c.OWN // 512):
            for rc in range(2):
                bi = rc
                bank = self.pb[bi]
                self.wait_g("tensor", self.bguard[bi])
                for ch in range(c.CCH):
                    self.emit("tensor", lambda e, bank=bank, ch=ch, rc=rc, tc=tc:
                              e.matmul(bank[:, :], self.u1s[:, ch, 128 * rc:128 * (rc + 1)],
                                       self.x1T[:, ch, 512 * tc:512 * (tc + 1)],
                                       start=(ch == 0), stop=(ch == c.CCH - 1)))
                tmm = self.inc_tag("tensor", "s_t")
                self.wait("scalar", "s_t", tmm)
                self.emit("scalar", lambda e, bank=bank, rc=rc, tc=tc: e.activation(
                    out=self.ptall[:, rc, 512 * tc:512 * (tc + 1)], in_=bank[:, :],
                    func=AF.Copy, bias=0.0, scale=1.0))
                av = self.inc_tag("scalar", "s_a")
                self.bguard[bi] = ("s_a", av)

        st_guard = [None, None]
        yt_guard = [None, None]
        for half in range(c.OWN // 1024):
            q2banks = [4, 5, 6, 7]
            for fc in range(c.FC):
                dbi = fc % 2
                self.wait_g("sync", st_guard[dbi])
                gv = self.dma("in", lambda e, fc=fc, dbi=dbi: e.dma_start(
                    out=self.v1st[:, dbi, :].rearrange("p (a b) -> p a b", a=2),
                    in_=self.d_v1[:, 128 * fc:128 * (fc + 1)].rearrange(
                        "(a p) b -> p a b", p=128)))
                gu = self.dma("in", lambda e, fc=fc, dbi=dbi: e.dma_start(
                    out=self.u2st[:, dbi, :], in_=self.d_u2[128 * fc:128 * (fc + 1), :]))
                hbi = 2 + dbi
                hbank = self.pb[hbi]
                self.wait_g("tensor", gv)
                self.wait_g("tensor", gu)
                last_h_mm = 0
                for tc2 in range(2):
                    self.wait_g("tensor", self.bguard[hbi])
                    for rc in range(2):
                        self.emit("tensor", lambda e, hbank=hbank, rc=rc, tc2=tc2,
                                  half=half, dbi=dbi:
                                  e.matmul(hbank[:, 0:512],
                                           self.v1st[:, dbi, 128 * rc:128 * (rc + 1)],
                                           self.ptall[:, rc, 1024 * half + 512 * tc2:
                                                      1024 * half + 512 * (tc2 + 1)],
                                           start=(rc == 0), stop=(rc == 1)))
                    last_h_mm = self.inc_tag("tensor", "s_t")
                    self.wait("scalar", "s_t", last_h_mm)
                    ag = 0
                    self.emit("scalar", lambda e, hbank=hbank, fc=fc, tc2=tc2:
                              e.activation(out=self.hts[:, tc2, :], in_=hbank[:, 0:512],
                                           func=AF.Gelu_apprx_tanh,
                                           bias=self.b1s[:, fc:fc + 1], scale=1.0))
                    ag = self.inc_tag("scalar", "s_a")
                    self.bguard[hbi] = ("s_a", ag)
                    self.wait("tensor", "s_a", ag)
                    for r2 in range(2):
                        qbi = 4 + 2 * tc2 + r2
                        if fc == 0:
                            self.wait_g("tensor", self.bguard[qbi])
                        self.emit("tensor", lambda e, qbi=qbi, r2=r2, fc=fc, tc2=tc2, dbi=dbi:
                                  e.matmul(self.pb[qbi][:, :],
                                           self.u2st[:, dbi, 128 * r2:128 * (r2 + 1)],
                                           self.hts[:, tc2, :],
                                           start=(fc == 0), stop=(fc == c.FC - 1)))
                        if r2 == 1:
                            last_h_mm = self.inc_tag("tensor", "s_t")
                st_guard[dbi] = ("s_t", last_h_mm)
            for tc2 in range(2):
                self.wait("scalar", "s_t", self.cnt["s_t"])
                aev = 0
                for r2 in range(2):
                    qbi = 4 + 2 * tc2 + r2
                    self.emit("scalar", lambda e, qbi=qbi, r2=r2: e.activation(
                        out=self.q2t[:, r2, :], in_=self.pb[qbi][:, :],
                        func=AF.Copy, bias=0.0, scale=1.0))
                    aev = self.inc_tag("scalar", "s_a")
                    self.bguard[qbi] = ("s_a", aev)
                for tt in range(4):
                    ti = 8 * half + 4 * tc2 + tt
                    ydbi = ti % 2
                    vln = 0
                    for dk in range(2):
                        ybi = dk
                        ybank = self.pb[ybi]
                        self.wait_g("tensor", self.bguard[ybi])
                        self.wait("tensor", "s_a", aev)
                        for r2 in range(2):
                            self.emit("tensor", lambda e, ybank=ybank, r2=r2, tt=tt, dk=dk:
                                      e.matmul(ybank[:, :],
                                               self.q2t[:, r2, 128 * tt:128 * (tt + 1)],
                                               self.v2s[:, r2, 512 * dk:512 * (dk + 1)],
                                               start=(r2 == 0), stop=False))
                        self.emit("tensor", lambda e, ybank=ybank, dk=dk:
                                  e.matmul(ybank[:, :], self.ones[0:1, :],
                                           self.b2s[0:1, 512 * dk:512 * (dk + 1)],
                                           start=False, stop=True))
                        tmm = self.inc_tag("tensor", "s_t")
                        self.wait("vector", "s_t", tmm)
                        self.emit("vector", lambda e, ybank=ybank, dk=dk, ti=ti, lnx=self.lnx4:
                                  e.tensor_tensor(lnx[:, 512 * dk:512 * (dk + 1)],
                                                  ybank[:, :],
                                                  self.x1tm[:, ti, 512 * dk:512 * (dk + 1)],
                                                  ALU.add))
                        v = self.inc_tag("vector", "s_v")
                        self.bguard[ybi] = ("s_v", v)
                    self.wait_g("vector", yt_guard[ydbi])
                    vln = self.layernorm(self.lnx4, self.ytile[:, ydbi, :], ti, None,
                                         sq=self.lnsq4)
                    self.wait("sync", "s_v", vln)
                    yt_guard[ydbi] = self.dma("y", lambda e, ti=ti, ydbi=ydbi: e.dma_start(
                        out=self.d_y[128 * ti:128 * (ti + 1), :], in_=self.ytile[:, ydbi, :]))
        self.wait_stream("sync", "y")


def _stage_core(inputs, cfg, b, r):
    c = cfg
    f32 = np.float32
    x = np.asarray(inputs["L0_attn_data"], dtype=f32)
    xb = np.ascontiguousarray(x[b])
    tiles = xb.reshape(c.NT, 128, c.DM)
    perm = np.arange(c.NT)
    if r == 1:
        perm = perm.reshape(-1, 2)[:, ::-1].ravel()
    x_full = np.ascontiguousarray(tiles[perm].reshape(c.T, c.DM))
    x_own = np.ascontiguousarray(tiles[perm[0::2]].reshape(c.OWN, c.DM))

    qkv_b = np.asarray(inputs["L0_attn_qkv_b"], dtype=f32)
    b1 = np.asarray(inputs["L0_ffn_b1"], dtype=f32)
    D = np.where(np.arange(128)[:, None] <= np.arange(128)[None, :],
                 f32(0.0), f32(NEG_BIG)).astype(f32)
    m1 = np.full((128, 128), NEG_BIG, f32) if r == 0 else np.zeros((128, 128), f32)
    return {
        "x_full": x_full,
        "x_own": x_own,
        "qkv_w": np.ascontiguousarray(np.asarray(inputs["L0_attn_qkv_w"], f32)),
        "qkv_b2d": np.ascontiguousarray(qkv_b.reshape(-1, 128).T),
        "out_w": np.ascontiguousarray(np.asarray(inputs["L0_attn_out_w"], f32)),
        "out_b2d": np.ascontiguousarray(np.asarray(inputs["L0_attn_out_b"], f32).reshape(1, -1)),
        "U1": np.ascontiguousarray(np.asarray(inputs["L0_ffn_U1"], f32)),
        "V1": np.ascontiguousarray(np.asarray(inputs["L0_ffn_V1"], f32)),
        "b1_2d": np.ascontiguousarray(b1.reshape(-1, 128).T),
        "U2": np.ascontiguousarray(np.asarray(inputs["L0_ffn_U2"], f32)),
        "V2": np.ascontiguousarray(np.asarray(inputs["L0_ffn_V2"], f32)),
        "b2_2d": np.ascontiguousarray(np.asarray(inputs["L0_ffn_b2"], f32).reshape(1, -1)),
        "c_ident": np.eye(128, dtype=f32),
        "c_ones": np.ones((128, 128), f32),
        "c_masks": np.ascontiguousarray(np.concatenate([D, m1], axis=1)),
    }


def _build(cfg):
    return B(cfg).build()


def kernel(**inputs):
    cfg = Cfg()
    nc = _build(cfg)
    in_maps = []
    meta = []
    for core in range(8):
        b, r = core // 2, core % 2
        in_maps.append(_stage_core(inputs, cfg, b, r))
        meta.append((b, r))
    res = run_bass_kernel_spmd(nc, in_maps, core_ids=list(range(8)))
    y = np.empty((4, cfg.T, cfg.DM), dtype=np.float32)
    for core, (b, r) in enumerate(meta):
        yo = np.asarray(res.results[core]["y_own"], dtype=np.float32)
        yv = y[b].reshape(cfg.NT, 128, cfg.DM)
        perm = np.arange(cfg.NT)
        if r == 1:
            perm = perm.reshape(-1, 2)[:, ::-1].ravel()
        yv[perm[0::2]] = yo.reshape(cfg.NT_OWN, 128, cfg.DM)
    return y
